# version 01
# baseline (speedup 1.0000x reference)
import numpy as np
from contextlib import ExitStack

import concourse.bass as bass
import concourse.mybir as mybir
from concourse.bass_utils import run_bass_kernel_spmd

F32 = mybir.dt.float32
AF = mybir.ActivationFunctionType
OP = mybir.AluOpType
AX = mybir.AxisListType

NCORES = 8
DM = 1024
L = 4096
TCH = 512
NCH = L // TCH

SEM_SWITCH = 24000


class Prog:
    def __init__(self, name="prog"):
        self.nc = bass.Bass("TRN2", target_bir_lowering=False)
        self.name = name
        self.ops = []
        self.cum = 0
        self.sem_idx = 0
        self.sem_totals = []
        self.stack = ExitStack()
        self._n = 0
        self._psum_pool = []
        self._psum_next = 0
        self._consts = {}

    def sbuf(self, shape, dtype=F32):
        self._n += 1
        return self.stack.enter_context(
            self.nc.sbuf_tensor(f"sb{self._n}", list(shape), dtype))

    def psum_pool_init(self, n=6):
        for i in range(n):
            self._psum_pool.append(self.stack.enter_context(
                self.nc.psum_tensor(f"ps{i}", [128, 512], F32)))

    def psum(self):
        t = self._psum_pool[self._psum_next % len(self._psum_pool)]
        self._psum_next += 1
        return t

    def dram(self, name, shape, dtype=F32, kind="Internal"):
        return self.nc.dram_tensor(name, list(shape), dtype, kind=kind)

    def emit(self, engine, fn, inc=1):
        if self.cum + inc > SEM_SWITCH:
            self.sem_totals.append(self.cum)
            self.sem_idx += 1
            self.cum = 0
        self.ops.append((engine, fn, inc, self.sem_idx, self.cum))
        self.cum += inc

    def ensure_const(self, v):
        v = float(v)
        if v in self._consts:
            return
        t = self.sbuf([128, 1])
        self.mset(t[:, :], v)
        self.nc.const_aps.register(F32, v, t[:, 0:1])
        self._consts[v] = t

    def dma(self, out, in_):
        self.emit("sync", lambda e, o=out, i=in_: e.dma_start(out=o, in_=i), 16)

    def mm(self, out, lhsT, rhs, start, stop):
        self.emit("tensor", lambda e, o=out, l=lhsT, r=rhs, s1=start, s2=stop:
                  e.matmul(o, l, r, start=s1, stop=s2))

    def trans(self, out_psum, in_, ident):
        self.emit("tensor", lambda e, o=out_psum, i=in_, d=ident:
                  e.transpose(o, i, d))

    def act(self, out, in_, func, bias=None, scale=None):
        kw = {}
        if bias is not None:
            if isinstance(bias, (int, float)):
                self.ensure_const(bias)
            kw["bias"] = bias
        if scale is not None:
            if isinstance(scale, (int, float)):
                self.ensure_const(scale)
            kw["scale"] = scale
        self.emit("scalar", lambda e, o=out, i=in_, f=func, k=kw:
                  e.activation(o, i, f, **k))

    def tt(self, out, in0, in1, op, eng="vector"):
        self.emit(eng, lambda e, o=out, a=in0, b=in1, p=op:
                  e.tensor_tensor(o, a, b, p))

    def ts(self, out, in0, s1, op0, s2=None, op1=None, eng="vector"):
        if op1 is None:
            self.emit(eng, lambda e, o=out, a=in0, x=s1, p=op0:
                      e.tensor_scalar(o, a, x, None, p))
        else:
            self.emit(eng, lambda e, o=out, a=in0, x=s1, y=s2, p=op0, q=op1:
                      e.tensor_scalar(o, a, x, y, p, q))

    def stt(self, out, in0, scalar, in1, op0, op1, eng="vector"):
        self.emit(eng, lambda e, o=out, a=in0, s=scalar, b=in1, p=op0, q=op1:
                  e.scalar_tensor_tensor(o, a, s, b, p, q))

    def red(self, out, in_, op="add", negate=False):
        if op == "add":
            self.emit("vector", lambda e, o=out, i=in_:
                      e.reduce_sum(o, i, axis=AX.X))
        else:
            self.emit("vector", lambda e, o=out, i=in_, n=negate:
                      e.reduce_max(o, i, axis=AX.X, negate=n))

    def copy(self, out, in_, eng="vector"):
        self.emit(eng, lambda e, o=out, i=in_: e.tensor_copy(o, i))

    def rcp(self, out, in_):
        self.emit("vector", lambda e, o=out, i=in_: e.reciprocal(o, i))

    def mset(self, ap, c):
        self.emit("vector", lambda e, a=ap, v=c: e.memset(a, v))

    def finalize(self):
        self.sem_totals.append(self.cum)
        nsems = self.sem_idx + 1
        sems = [self.stack.enter_context(self.nc.semaphore(f"tick{i}"))
                for i in range(nsems)]
        used = []
        for (en, _, _, _, _) in self.ops:
            if en not in used:
                used.append(en)
        ops = self.ops
        totals = self.sem_totals

        with self.nc.Block() as block:
            for en in used:
                def make(name):
                    def run(eng):
                        last = 0
                        for (e, fn, inc, si, wv) in ops:
                            if e != name:
                                continue
                            while last < si:
                                eng.wait_ge(sems[last], totals[last])
                                last += 1
                            eng.wait_ge(sems[si], wv)
                            fn(eng).then_inc(sems[si], inc)
                        while last < nsems - 1:
                            eng.wait_ge(sems[last], totals[last])
                            last += 1
                        eng.wait_ge(sems[nsems - 1], totals[nsems - 1])
                    return run
                getattr(block, en)(make(en))
        return self.nc



def load_const_tiles(p, dram_h, nt):
    sb = p.sbuf([128, nt])
    for t in range(nt):
        p.dma(sb[:, t:t + 1], dram_h.ap()[t:t + 1, :])
    return sb


def emit_ln(p, x_tiles, out_tiles, ones_mm, W, nf=8, inv=None, eps=1e-5):
    if inv is None:
        inv = 1.0 / (128 * nf)
    s1 = p.psum()
    s2 = p.psum()
    if not hasattr(p, "_ln_ws"):
        p._ln_ws = {}
    if W not in p._ln_ws:
        p._ln_ws[W] = [p.sbuf([128, W]) for _ in range(5)]
    sq, m1, m2, v, istd = p._ln_ws[W]
    mi = v
    for f in range(nf):
        p.mm(s1[:, :W], ones_mm[:, :], x_tiles[f], start=(f == 0), stop=(f == nf - 1))
    for f in range(nf):
        p.act(sq[:, :W], x_tiles[f], AF.Square)
        p.mm(s2[:, :W], ones_mm[:, :], sq[:, :W], start=(f == 0), stop=(f == nf - 1))
    p.ts(m1[:, :W], s1[:, :W], inv, OP.mult)
    p.ts(m2[:, :W], s2[:, :W], inv, OP.mult)
    p.tt(v[:, :W], m1[:, :W], m1[:, :W], OP.mult)
    p.tt(v[:, :W], m2[:, :W], v[:, :W], OP.subtract)
    p.act(istd[:, :W], v[:, :W], AF.Ln, bias=eps)
    p.act(istd[:, :W], istd[:, :W], AF.Exp, scale=-0.5)
    p.tt(mi[:, :W], m1[:, :W], istd[:, :W], OP.mult)
    for f in range(nf):
        p.tt(out_tiles[f], x_tiles[f], istd[:, :W], OP.mult)
        p.tt(out_tiles[f], out_tiles[f], mi[:, :W], OP.subtract)


def emit_matmul_ff(p, wT_dram, x_tiles, W, KT, MT, out_cb, wbuf=None):
    if wbuf is None:
        wbuf = p.sbuf([128, 128])
    for m in range(MT):
        ps = p.psum()
        for k in range(KT):
            p.dma(wbuf[:, :], wT_dram.ap()[k * 128:(k + 1) * 128,
                                           m * 128:(m + 1) * 128])
            p.mm(ps[:, :W], wbuf[:, :], x_tiles[k], start=(k == 0), stop=(k == KT - 1))
        out_cb(m, ps[:, :W])



def build_p1():
    p = Prog("p1")
    x_d = p.dram("x", [DM, L], kind="ExternalInput")
    w_d = p.dram("wip", [DM, 8768], kind="ExternalInput")
    ones_d = p.dram("ones", [128, 128], kind="ExternalInput")
    zx_d = p.dram("zx", [8768, L], kind="ExternalOutput")
    p.psum_pool_init(6)
    ones_sb = p.sbuf([128, 128])
    p.dma(ones_sb[:, :], ones_d.ap()[:, :])
    xt = [p.sbuf([128, TCH]) for _ in range(8)]
    xn = [p.sbuf([128, TCH]) for _ in range(8)]
    wsb = p.sbuf([128, 128])
    osb = p.sbuf([128, TCH])
    mblocks = [(m * 128, 128) for m in range(68)] + [(8704, 64)]
    for c in range(NCH):
        c0 = c * TCH
        for f in range(8):
            p.dma(xt[f][:, :], x_d.ap()[f * 128:(f + 1) * 128, c0:c0 + TCH])
        emit_ln(p, [t[:, :] for t in xt], [t[:, :] for t in xn], ones_sb, TCH)
        for (m0, mw) in mblocks:
            ps = p.psum()
            for k in range(8):
                p.dma(wsb[:, 0:mw], w_d.ap()[k * 128:(k + 1) * 128, m0:m0 + mw])
                p.mm(ps[0:mw, 0:TCH], wsb[:, 0:mw], xn[k][:, :], start=(k == 0), stop=(k == 7))
            p.copy(osb[0:mw, :], ps[0:mw, 0:TCH])
            p.dma(zx_d.ap()[m0:m0 + mw, c0:c0 + TCH], osb[0:mw, :])
    p.finalize()
    return p


def build_p2():
    p = Prog("p2")
    zx_d = p.dram("zxb", [4384, L], kind="ExternalInput")
    convw_d = p.dram("convw", [2304, 4], kind="ExternalInput")
    convb_d = p.dram("convb", [18, 128], kind="ExternalInput")
    dtb_d = p.dram("dtb", [32, 1], kind="ExternalInput")
    alog_d = p.dram("alog", [32, 1], kind="ExternalInput")
    drep_d = p.dram("drep", [16, 128], kind="ExternalInput")
    normw_d = p.dram("normw", [16, 128], kind="ExternalInput")
    opw_d = p.dram("opwT", [2048, DM], kind="ExternalInput")
    ones_d = p.dram("ones", [128, 128], kind="ExternalInput")
    eye_d = p.dram("eye", [128, 128], kind="ExternalInput")
    mask_d = p.dram("maskUT", [128, 512], kind="ExternalInput")
    sel_d = p.dram("sel32", [32, 4096], kind="ExternalInput")
    ob_d = p.dram("ob", [DM, L], kind="ExternalOutput")
    xbcc_d = p.dram("xbcc", [2304, L], kind="Internal")
    p.psum_pool_init(6)

    ones_sb = p.sbuf([128, 128]); p.dma(ones_sb[:, :], ones_d.ap()[:, :])
    eye_sb = p.sbuf([128, 128]); p.dma(eye_sb[:, :], eye_d.ap()[:, :])
    mask_sb = p.sbuf([128, 512]); p.dma(mask_sb[:, :], mask_d.ap()[:, :])
    sel_sb = p.sbuf([32, 4096]); p.dma(sel_sb[:, :], sel_d.ap()[:, :])
    drep_sb = load_const_tiles(p, drep_d, 16)
    normw_sb = load_const_tiles(p, normw_d, 16)
    convb_sb = load_const_tiles(p, convb_d, 18)

    dtraw = p.sbuf([32, L]); p.dma(dtraw[:, :], zx_d.ap()[4352:4384, :])
    dtb_sb = p.sbuf([32, 1]); p.dma(dtb_sb[:, :], dtb_d.ap()[:, :])
    alog_sb = p.sbuf([32, 1]); p.dma(alog_sb[:, :], alog_d.ap()[:, :])
    a_sb = p.sbuf([32, 1])
    p.act(a_sb[:, :], alog_sb[:, :], AF.Exp)
    p.ts(a_sb[:, :], a_sb[:, :], -1.0, OP.mult)
    dt_sb = p.sbuf([32, L])
    p.act(dt_sb[:, :], dtraw[:, :], AF.Exp, bias=dtb_sb[:, 0:1])
    p.act(dt_sb[:, :], dt_sb[:, :], AF.Ln, bias=1.0)
    da_sb = p.sbuf([32, L])
    p.ts(da_sb[:, :], dt_sb[:, :], a_sb[:, 0:1], OP.mult)
    cs_all = p.sbuf([32, L])
    cw0 = p.sbuf([32, 256]); cw1 = p.sbuf([32, 256])
    for c in range(16):
        c0 = c * 256
        p.copy(cw0[:, :], da_sb[:, c0:c0 + 256])
        src, dst = cw0, cw1
        for st in [1, 2, 4, 8, 16, 32, 64, 128]:
            p.tt(dst[:, st:256], src[:, st:256], src[:, 0:256 - st], OP.add)
            p.copy(dst[:, 0:st], src[:, 0:st])
            src, dst = dst, src
        p.copy(cs_all[:, c0:c0 + 256], src[:, :])

    wconv_sb = p.sbuf([128, 72])
    for ft in range(18):
        p.dma(wconv_sb[:, ft * 4:(ft + 1) * 4], convw_d.ap()[ft * 128:(ft + 1) * 128, :])
    xc_sb = p.sbuf([128, 515])
    acc0 = p.sbuf([128, 512]); acc1 = p.sbuf([128, 512])
    xo_sb = p.sbuf([128, 512])
    for c in range(NCH):
        c0 = c * TCH
        for ft in range(18):
            r0 = 2048 + ft * 128
            if c == 0:
                p.mset(xc_sb[:, 0:3], 0.0)
                p.dma(xc_sb[:, 3:515], zx_d.ap()[r0:r0 + 128, 0:512])
            else:
                p.dma(xc_sb[:, :], zx_d.ap()[r0:r0 + 128, c0 - 3:c0 + 512])
            wc = wconv_sb[:, ft * 4:ft * 4 + 4]
            p.ts(acc0[:, :], xc_sb[:, 0:512], wconv_sb[:, ft * 4:ft * 4 + 1], OP.mult)
            p.stt(acc1[:, :], xc_sb[:, 1:513], wconv_sb[:, ft * 4 + 1:ft * 4 + 2], acc0[:, :], OP.mult, OP.add)
            p.stt(acc0[:, :], xc_sb[:, 2:514], wconv_sb[:, ft * 4 + 2:ft * 4 + 3], acc1[:, :], OP.mult, OP.add)
            p.stt(acc1[:, :], xc_sb[:, 3:515], wconv_sb[:, ft * 4 + 3:ft * 4 + 4], acc0[:, :], OP.mult, OP.add)
            p.act(xo_sb[:, :], acc1[:, :], AF.Silu, bias=convb_sb[:, ft:ft + 1])
            p.dma(xbcc_d.ap()[ft * 128:(ft + 1) * 128, c0:c0 + TCH], xo_sb[:, :])

    hs_sb = p.sbuf([128, 2048])
    p.mset(hs_sb[:, :], 0.0)
    xs_sb = p.sbuf([128, 4096])
    ybuf = p.sbuf([128, 4096])
    xtok = [p.sbuf([128, 2048]) for _ in range(2)]
    btok = [p.sbuf([128, 128]) for _ in range(2)]
    bt_sb = p.sbuf([128, 256]); ct_sb = p.sbuf([128, 256])
    g_sb = [p.sbuf([128, 256]) for _ in range(2)]
    csT = [p.sbuf([128, 32]) for _ in range(2)]
    ncsT = [p.sbuf([128, 32]) for _ in range(2)]
    dtT = [p.sbuf([128, 32]) for _ in range(2)]
    csb_sb = p.sbuf([128, 256]); ecs_sb = p.sbuf([128, 256]); cexp_sb = p.sbuf([128, 256])
    arg_sb = p.sbuf([128, 256]); lexp_sb = p.sbuf([128, 256])
    mt_sb = [p.sbuf([128, 256]) for _ in range(2)]
    dcol_sb = p.sbuf([128, 2])
    sin_sb = [p.sbuf([128, 64]) for _ in range(2)]
    cd_sb = p.sbuf([128, 1])
    ytmp = p.sbuf([128, 256])
    sq_sb = p.sbuf([128, 256])
    istd_sb = p.sbuf([128, 256])
    zsil = p.sbuf([128, 256])
    wsb = p.sbuf([128, 128])
    osb = p.sbuf([128, 256])

    CH = 256
    for c in range(16):
        c0 = c * CH
        for ft in range(16):
            p.dma(xs_sb[:, ft * 256:(ft + 1) * 256], xbcc_d.ap()[ft * 128:(ft + 1) * 128, c0:c0 + CH])
        p.dma(bt_sb[:, :], xbcc_d.ap()[2048:2176, c0:c0 + CH])
        p.dma(ct_sb[:, :], xbcc_d.ap()[2176:2304, c0:c0 + CH])
        for ft in range(16):
            for lt in range(2):
                ps = p.psum()
                p.trans(ps[:, 0:128], xs_sb[:, ft * 256 + lt * 128:ft * 256 + (lt + 1) * 128], eye_sb[:, :])
                p.copy(xtok[lt][:, ft * 128:(ft + 1) * 128], ps[:, 0:128])
        for lt in range(2):
            ps = p.psum()
            p.trans(ps[:, 0:128], bt_sb[:, lt * 128:(lt + 1) * 128], eye_sb[:, :])
            p.copy(btok[lt][:, :], ps[:, 0:128])
            ps2 = p.psum()
            p.mm(ps2[:, 0:256], bt_sb[:, lt * 128:(lt + 1) * 128], ct_sb[:, :], start=True, stop=True)
            p.copy(g_sb[lt][:, :], ps2[:, 0:256])
            ps3 = p.psum()
            p.trans(ps3[0:128, 0:32], cs_all[:, c0 + lt * 128:c0 + (lt + 1) * 128], eye_sb[0:32, 0:32])
            p.copy(csT[lt][:, :], ps3[0:128, 0:32])
            p.ts(ncsT[lt][:, :], csT[lt][:, :], -1.0, OP.mult)
            ps4 = p.psum()
            p.trans(ps4[0:128, 0:32], dt_sb[:, c0 + lt * 128:c0 + (lt + 1) * 128], eye_sb[0:32, 0:32])
            p.copy(dtT[lt][:, :], ps4[0:128, 0:32])
        for hp in range(16):
            yps = p.psum()
            for sub in range(2):
                h = 2 * hp + sub
                b0 = sub * 64
                psC = p.psum()
                p.mm(psC[:, 0:256], sel_sb[:, h * 128:(h + 1) * 128], cs_all[:, c0:c0 + CH], start=True, stop=True)
                p.copy(csb_sb[:, :], psC[:, 0:256])
                p.act(ecs_sb[:, :], csb_sb[:, :], AF.Exp)
                p.tt(cexp_sb[:, :], ct_sb[:, :], ecs_sb[:, :], OP.mult)
                for lt in range(2):
                    off = 128 - lt * 128
                    p.tt(arg_sb[:, :], csb_sb[:, :], mask_sb[:, off:off + 256], OP.add)
                    p.act(lexp_sb[:, :], arg_sb[:, :], AF.Exp, bias=ncsT[lt][:, h:h + 1])
                    p.tt(mt_sb[lt][:, :], lexp_sb[:, :], g_sb[lt][:, :], OP.mult)
                    p.ts(mt_sb[lt][:, :], mt_sb[lt][:, :], dtT[lt][:, h:h + 1], OP.mult)
                    p.mm(yps[b0:b0 + 64, 0:256], xtok[lt][:, h * 64:(h + 1) * 64], mt_sb[lt][:, :],
                         start=(lt == 0), stop=False)
                    p.act(dcol_sb[:, lt:lt + 1], ncsT[lt][:, h:h + 1], AF.Exp, bias=csb_sb[:, 255:256])
                    p.tt(dcol_sb[:, lt:lt + 1], dcol_sb[:, lt:lt + 1], dtT[lt][:, h:h + 1], OP.mult)
                    p.ts(sin_sb[lt][:, :], xtok[lt][:, h * 64:(h + 1) * 64], dcol_sb[:, lt:lt + 1], OP.mult)
                psS = p.psum()
                for lt in range(2):
                    p.mm(psS[:, 0:64], btok[lt][:, :], sin_sb[lt][:, :], start=(lt == 0), stop=(lt == 1))
                p.mm(yps[b0:b0 + 64, 0:256], hs_sb[:, h * 64:(h + 1) * 64], cexp_sb[:, :], start=False, stop=True)
                p.act(cd_sb[:, :], csb_sb[:, 255:256], AF.Exp)
                p.ts(hs_sb[:, h * 64:(h + 1) * 64], hs_sb[:, h * 64:(h + 1) * 64], cd_sb[:, 0:1], OP.mult)
                p.tt(hs_sb[:, h * 64:(h + 1) * 64], hs_sb[:, h * 64:(h + 1) * 64], psS[:, 0:64], OP.add)
            p.ts(ytmp[:, :], xs_sb[:, hp * 256:(hp + 1) * 256], drep_sb[:, hp:hp + 1], OP.mult)
            p.tt(ybuf[:, hp * 256:(hp + 1) * 256], ytmp[:, :], yps[:, 0:256], OP.add)
        for ft in range(16):
            p.dma(xs_sb[:, ft * 256:(ft + 1) * 256], zx_d.ap()[ft * 128:(ft + 1) * 128, c0:c0 + CH])
        psR = p.psum()
        for ft in range(16):
            blk = ybuf[:, ft * 256:(ft + 1) * 256]
            p.act(zsil[:, :], xs_sb[:, ft * 256:(ft + 1) * 256], AF.Silu)
            p.tt(blk, blk, zsil[:, :], OP.mult)
            p.act(sq_sb[:, :], blk, AF.Square)
            p.mm(psR[:, 0:256], ones_sb[:, :], sq_sb[:, :], start=(ft == 0), stop=(ft == 15))
        p.act(istd_sb[:, :], psR[:, 0:256], AF.Ln, scale=1.0 / 2048.0, bias=1e-5)
        p.act(istd_sb[:, :], istd_sb[:, :], AF.Exp, scale=-0.5)
        for ft in range(16):
            blk = ybuf[:, ft * 256:(ft + 1) * 256]
            p.tt(blk, blk, istd_sb[:, :], OP.mult)
            p.ts(blk, blk, normw_sb[:, ft:ft + 1], OP.mult)
        for m in range(8):
            ps = p.psum()
            for k in range(16):
                p.dma(wsb[:, :], opw_d.ap()[k * 128:(k + 1) * 128, m * 128:(m + 1) * 128])
                p.mm(ps[:, 0:256], wsb[:, :], ybuf[:, k * 256:(k + 1) * 256], start=(k == 0), stop=(k == 15))
            p.copy(osb[:, :], ps[:, 0:256])
            p.dma(ob_d.ap()[m * 128:(m + 1) * 128, c0:c0 + CH], osb[:, :])
    p.finalize()
    return p


def build_p4():
    p = Prog("p4")
    x_d = p.dram("x", [DM, L], kind="ExternalInput")
    of_d = p.dram("of", [DM, L], kind="ExternalInput")
    obk_d = p.dram("obk", [DM, L], kind="ExternalInput")
    ew_d = p.dram("endwT", [DM, DM], kind="ExternalInput")
    eb_d = p.dram("endb", [8, 128], kind="ExternalInput")
    w1_d = p.dram("w1col", [128, 1], kind="ExternalInput")
    w2_d = p.dram("w2col", [128, 1], kind="ExternalInput")
    xo_d = p.dram("xo", [DM, L], kind="ExternalOutput")
    p.psum_pool_init(6)
    eb_sb = load_const_tiles(p, eb_d, 8)
    w1_sb = p.sbuf([128, 1]); p.dma(w1_sb[:, :], w1_d.ap()[:, :])
    w2_sb = p.sbuf([128, 1]); p.dma(w2_sb[:, :], w2_d.ap()[:, :])
    cb = [p.sbuf([128, TCH]) for _ in range(8)]
    tmp = p.sbuf([128, TCH])
    wsb = p.sbuf([128, 128])
    osb = p.sbuf([128, TCH])
    for c in range(NCH):
        c0 = c * TCH
        for f in range(8):
            p.dma(cb[f][:, :], of_d.ap()[f * 128:(f + 1) * 128, c0:c0 + TCH])
            p.ts(cb[f][:, :], cb[f][:, :], w1_sb[:, 0:1], OP.mult)
            p.dma(tmp[:, :], obk_d.ap()[f * 128:(f + 1) * 128, c0:c0 + TCH])
            p.stt(cb[f][:, :], tmp[:, :], w2_sb[:, 0:1], cb[f][:, :], OP.mult, OP.add)
        for m in range(8):
            ps = p.psum()
            for k in range(8):
                p.dma(wsb[:, :], ew_d.ap()[k * 128:(k + 1) * 128, m * 128:(m + 1) * 128])
                p.mm(ps[:, :TCH], wsb[:, :], cb[k][:, :], start=(k == 0), stop=(k == 7))
            p.act(osb[:, :], ps[:, :TCH], AF.Identity, bias=eb_sb[:, m:m + 1])
            p.dma(tmp[:, :], x_d.ap()[m * 128:(m + 1) * 128, c0:c0 + TCH])
            p.tt(osb[:, :], osb[:, :], tmp[:, :], OP.add)
            p.dma(xo_d.ap()[m * 128:(m + 1) * 128, c0:c0 + TCH], osb[:, :])
    p.finalize()
    return p


def build_p5():
    p = Prog("p5")
    x_d = p.dram("x", [DM, L], kind="ExternalInput")
    wr_d = p.dram("wrT", [DM, DM], kind="ExternalInput")
    wk_d = p.dram("wkT", [DM, DM], kind="ExternalInput")
    wv_d = p.dram("wvT", [DM, DM], kind="ExternalInput")
    mur_d = p.dram("mur", [8, 128], kind="ExternalInput")
    muk_d = p.dram("muk", [8, 128], kind="ExternalInput")
    ones_d = p.dram("ones", [128, 128], kind="ExternalInput")
    xo_d = p.dram("xo", [DM, L], kind="ExternalOutput")
    p.psum_pool_init(6)
    ones_sb = p.sbuf([128, 128]); p.dma(ones_sb[:, :], ones_d.ap()[:, :])
    mur_sb = load_const_tiles(p, mur_d, 8)
    muk_sb = load_const_tiles(p, muk_d, 8)
    omur_sb = p.sbuf([128, 8]); omuk_sb = p.sbuf([128, 8])
    p.ts(omur_sb[:, :], mur_sb[:, :], -1.0, OP.mult, 1.0, OP.add)
    p.ts(omuk_sb[:, :], muk_sb[:, :], -1.0, OP.mult, 1.0, OP.add)
    xc = [p.sbuf([128, TCH]) for _ in range(8)]
    xp = [p.sbuf([128, TCH]) for _ in range(8)]
    xnc = [p.sbuf([128, TCH]) for _ in range(8)]
    xnp = [p.sbuf([128, TCH]) for _ in range(8)]
    xr = [p.sbuf([128, TCH]) for _ in range(8)]
    xk = [p.sbuf([128, TCH]) for _ in range(8)]
    r_t = [p.sbuf([128, TCH]) for _ in range(8)]
    ka = [p.sbuf([128, TCH]) for _ in range(8)]
    tmp = p.sbuf([128, TCH])
    for c in range(NCH):
        c0 = c * TCH
        for f in range(8):
            p.dma(xc[f][:, :], x_d.ap()[f * 128:(f + 1) * 128, c0:c0 + TCH])
            if c == 0:
                p.mset(xp[f][:, 0:1], 0.0)
                p.dma(xp[f][:, 1:TCH], x_d.ap()[f * 128:(f + 1) * 128, 0:TCH - 1])
            else:
                p.dma(xp[f][:, :], x_d.ap()[f * 128:(f + 1) * 128, c0 - 1:c0 + TCH - 1])
        emit_ln(p, [t[:, :] for t in xc], [t[:, :] for t in xnc], ones_sb, TCH)
        emit_ln(p, [t[:, :] for t in xp], [t[:, :] for t in xnp], ones_sb, TCH)
        for f in range(8):
            p.ts(tmp[:, :], xnc[f][:, :], mur_sb[:, f:f + 1], OP.mult)
            p.stt(xr[f][:, :], xnp[f][:, :], omur_sb[:, f:f + 1], tmp[:, :], OP.mult, OP.add)
            p.ts(tmp[:, :], xnc[f][:, :], muk_sb[:, f:f + 1], OP.mult)
            p.stt(xk[f][:, :], xnp[f][:, :], omuk_sb[:, f:f + 1], tmp[:, :], OP.mult, OP.add)
        def cb_r(m, ps):
            p.act(r_t[m][:, :], ps, AF.Sigmoid)
        emit_matmul_ff(p, wr_d, [t[:, :] for t in xr], TCH, 8, 8, cb_r)
        def cb_k(m, ps):
            p.act(ka[m][:, :], ps, AF.Relu)
            p.act(ka[m][:, :], ka[m][:, :], AF.Square)
        emit_matmul_ff(p, wk_d, [t[:, :] for t in xk], TCH, 8, 8, cb_k)
        def cb_v(m, ps):
            p.tt(tmp[:, :], r_t[m][:, :], ps, OP.mult)
            p.tt(tmp[:, :], tmp[:, :], xc[m][:, :], OP.add)
            p.dma(xo_d.ap()[m * 128:(m + 1) * 128, c0:c0 + TCH], tmp[:, :])
        emit_matmul_ff(p, wv_d, [t[:, :] for t in ka], TCH, 8, 8, cb_v)
    p.finalize()
    return p


def build_p6():
    p = Prog("p6")
    H, DH = 16, 64
    x_d = p.dram("x", [DM, L], kind="ExternalInput")
    wkqv_d = p.dram("wkqv", [DM, 3 * DM], kind="ExternalInput")
    bkq_d = p.dram("bkq", [16, 128], kind="ExternalInput")
    bv_d = p.dram("bvrow", [1, DM], kind="ExternalInput")
    wp_d = p.dram("wproj", [DM, DM], kind="ExternalInput")
    bp_d = p.dram("bproj", [8, 128], kind="ExternalInput")
    slp_d = p.dram("slopes", [1, 16], kind="ExternalInput")
    iota_d = p.dram("iotarow", [1, L], kind="ExternalInput")
    ic_d = p.dram("iotacol", [128, 1], kind="ExternalInput")
    mb_d = p.dram("maskBig", [128, 896], kind="ExternalInput")
    ones_d = p.dram("ones", [128, 128], kind="ExternalInput")
    xo_d = p.dram("xo", [DM, L], kind="ExternalOutput")
    kq_s = p.dram("kq_scr", [2 * DM, L], kind="Internal")
    v_s = p.dram("v_scr", [H, L, DH], kind="Internal")
    y_s = p.dram("y_scr", [DM, L], kind="Internal")
    p.psum_pool_init(6)
    ones_sb = p.sbuf([128, 128]); p.dma(ones_sb[:, :], ones_d.ap()[:, :])
    bkq_sb = load_const_tiles(p, bkq_d, 16)
    bp_sb = load_const_tiles(p, bp_d, 8)
    mb_sb = p.sbuf([128, 896]); p.dma(mb_sb[:, :], mb_d.ap()[:, :])
    ic_sb = p.sbuf([128, 1]); p.dma(ic_sb[:, :], ic_d.ap()[:, :])
    slr = p.sbuf([1, 16]); p.dma(slr[:, :], slp_d.ap()[:, :])
    ior = p.sbuf([1, L]); p.dma(ior[:, :], iota_d.ap()[:, :])
    bvr = p.sbuf([1, DM]); p.dma(bvr[:, :], bv_d.ap()[:, :])
    slp_sb = p.sbuf([128, 16])
    ps = p.psum()
    p.mm(ps[:, 0:16], ones_sb[0:1, :], slr[:, :], start=True, stop=True)
    p.copy(slp_sb[:, :], ps[:, 0:16])
    jb_sb = p.sbuf([128, L])
    for q in range(8):
        ps = p.psum()
        p.mm(ps[:, 0:512], ones_sb[0:1, :], ior[:, q * 512:(q + 1) * 512], start=True, stop=True)
        p.copy(jb_sb[:, q * 512:(q + 1) * 512], ps[:, 0:512])
    bvb_sb = p.sbuf([128, DM])
    for q in range(2):
        ps = p.psum()
        p.mm(ps[:, 0:512], ones_sb[0:1, :], bvr[:, q * 512:(q + 1) * 512], start=True, stop=True)
        p.copy(bvb_sb[:, q * 512:(q + 1) * 512], ps[:, 0:512])

    xt = [p.sbuf([128, TCH]) for _ in range(8)]
    xn = [p.sbuf([128, TCH]) for _ in range(8)]
    wsb = p.sbuf([128, 128])
    wv_sb = p.sbuf([128, 512])
    osb = p.sbuf([128, TCH])
    vsb = p.sbuf([128, 512])
    for c in range(NCH):
        c0 = c * TCH
        for f in range(8):
            p.dma(xt[f][:, :], x_d.ap()[f * 128:(f + 1) * 128, c0:c0 + TCH])
        emit_ln(p, [t[:, :] for t in xt], [t[:, :] for t in xn], ones_sb, TCH)
        for m in range(16):
            ps = p.psum()
            for k in range(8):
                p.dma(wsb[:, :], wkqv_d.ap()[k * 128:(k + 1) * 128, m * 128:(m + 1) * 128])
                p.mm(ps[:, :TCH], wsb[:, :], xn[k][:, :], start=(k == 0), stop=(k == 7))
            p.act(osb[:, :], ps[:, :TCH], AF.Identity, bias=bkq_sb[:, m:m + 1])
            p.dma(kq_s.ap()[m * 128:(m + 1) * 128, c0:c0 + TCH], osb[:, :])
        for tb in range(4):
            for vc in range(2):
                ps = p.psum()
                for k in range(8):
                    p.dma(wv_sb[:, :], wkqv_d.ap()[k * 128:(k + 1) * 128,
                                                   2048 + vc * 512:2048 + (vc + 1) * 512])
                    p.mm(ps[:, 0:512], xn[k][:, tb * 128:(tb + 1) * 128], wv_sb[:, :],
                         start=(k == 0), stop=(k == 7))
                p.tt(vsb[:, :], ps[:, 0:512], bvb_sb[:, vc * 512:(vc + 1) * 512], OP.add)
                for hh in range(8):
                    h = vc * 8 + hh
                    p.dma(v_s.ap()[h, c0 + tb * 128:c0 + (tb + 1) * 128, :],
                          vsb[:, hh * 64:(hh + 1) * 64])
    kT = p.sbuf([64, L]); qT = p.sbuf([64, L]); vT = p.sbuf([128, 32 * 64])
    jh_sb = p.sbuf([128, L])
    jcol = p.sbuf([128, 1])
    psb = p.sbuf([128, 512])
    rd_sb = p.sbuf([64, 512])
    y_sb = p.sbuf([64, 512])
    for h in range(H):
        p.dma(kT[:, :], kq_s.ap()[h * 64:(h + 1) * 64, :])
        p.dma(qT[:, :], kq_s.ap()[DM + h * 64:DM + (h + 1) * 64, :])
        for t in range(32):
            p.dma(vT[:, t * 64:(t + 1) * 64], v_s.ap()[h, t * 128:(t + 1) * 128, :])
        p.ts(jh_sb[:, :], jb_sb[:, :], slp_sb[:, h:h + 1], OP.mult)
        for qc in range(8):
            q0 = qc * 512
            njb = 4 * qc + 4
            d_ps = p.psum()
            y_ps = p.psum()
            s_ps = p.psum()
            for jb in range(njb):
                p.mm(s_ps[:, 0:512], kT[:, jb * 128:(jb + 1) * 128], qT[:, q0:q0 + 512],
                     start=True, stop=True)
                p.ts(jcol[:, :], ic_sb[:, 0:1], float(jb * 128), OP.add)
                p.tt(jcol[:, :], jcol[:, :], slp_sb[:, h:h + 1], OP.mult)
                p.act(psb[:, :], s_ps[:, 0:512], AF.Identity, bias=jcol[:, 0:1], scale=0.125)
                p.tt(psb[:, :], psb[:, :], jh_sb[:, q0:q0 + 512], OP.subtract)
                if jb >= 4 * qc:
                    r = jb * 128 - q0
                    s = 384 - r
                    p.tt(psb[:, :], psb[:, :], mb_sb[:, s:s + 512], OP.add)
                p.act(psb[:, :], psb[:, :], AF.Exp)
                p.mm(d_ps[0:64, 0:512], ones_sb[:, 0:64], psb[:, :],
                     start=(jb == 0), stop=(jb == njb - 1))
                p.mm(y_ps[0:64, 0:512], vT[:, jb * 64:(jb + 1) * 64], psb[:, :],
                     start=(jb == 0), stop=(jb == njb - 1))
            p.rcp(rd_sb[:, :], d_ps[0:64, 0:512])
            p.tt(y_sb[:, :], y_ps[0:64, 0:512], rd_sb[:, :], OP.mult)
            p.dma(y_s.ap()[h * 64:(h + 1) * 64, q0:q0 + 512], y_sb[:, :])
    yt = [p.sbuf([128, TCH]) for _ in range(8)]
    tmp = p.sbuf([128, TCH])
    for c in range(NCH):
        c0 = c * TCH
        for f in range(8):
            p.dma(yt[f][:, :], y_s.ap()[f * 128:(f + 1) * 128, c0:c0 + TCH])
        for m in range(8):
            ps = p.psum()
            for k in range(8):
                p.dma(wsb[:, :], wp_d.ap()[k * 128:(k + 1) * 128, m * 128:(m + 1) * 128])
                p.mm(ps[:, :TCH], wsb[:, :], yt[k][:, :], start=(k == 0), stop=(k == 7))
            p.act(osb[:, :], ps[:, :TCH], AF.Identity, bias=bp_sb[:, m:m + 1])
            p.dma(tmp[:, :], x_d.ap()[m * 128:(m + 1) * 128, c0:c0 + TCH])
            p.tt(osb[:, :], osb[:, :], tmp[:, :], OP.add)
            p.dma(xo_d.ap()[m * 128:(m + 1) * 128, c0:c0 + TCH], osb[:, :])
    p.finalize()
    return p


def build_p7():
    p = Prog("p7")
    x_d = p.dram("x", [DM, L], kind="ExternalInput")
    gw1_d = p.dram("gw1T", [DM, DM], kind="ExternalInput")
    gb1_d = p.dram("gb1", [8, 128], kind="ExternalInput")
    gw2_d = p.dram("gw2T", [DM, 8], kind="ExternalInput")
    gb2_d = p.dram("gb2", [8, 1], kind="ExternalInput")
    w1_d = p.dram("w1T", [DM, 4096], kind="ExternalInput")
    tw1_d = p.dram("tw1T", [8, DM, 4096], kind="ExternalInput")
    w2_d = p.dram("w2T", [4096, DM], kind="ExternalInput")
    tw2_d = p.dram("tw2T", [8, 4096, DM], kind="ExternalInput")
    tb1_d = p.dram("tb1", [8, 4096], kind="ExternalInput")
    b1_d = p.dram("b1c", [32, 128], kind="ExternalInput")
    tb2_d = p.dram("tb2", [8, DM], kind="ExternalInput")
    b2_d = p.dram("b2c", [8, 128], kind="ExternalInput")
    ones_d = p.dram("ones", [128, 128], kind="ExternalInput")
    eye_d = p.dram("eye", [128, 128], kind="ExternalInput")
    xo_d = p.dram("xo", [DM, L], kind="ExternalOutput")
    mw1_s = p.dram("mw1_scr", [DM, 4096], kind="Internal")
    mw2_s = p.dram("mw2_scr", [4096, DM], kind="Internal")
    p.psum_pool_init(6)
    ones_sb = p.sbuf([128, 128]); p.dma(ones_sb[:, :], ones_d.ap()[:, :])
    eye_sb = p.sbuf([128, 128]); p.dma(eye_sb[:, :], eye_d.ap()[:, :])
    gb1_sb = load_const_tiles(p, gb1_d, 8)
    b1_sb = load_const_tiles(p, b1_d, 32)
    b2_sb = load_const_tiles(p, b2_d, 8)
    gb2_sb = p.sbuf([8, 1]); p.dma(gb2_sb[:, :], gb2_d.ap()[:, :])
    gw2_sb = p.sbuf([128, 64])
    for k in range(8):
        p.dma(gw2_sb[:, k * 8:(k + 1) * 8], gw2_d.ap()[k * 128:(k + 1) * 128, :])
    tb1_sb = p.sbuf([8, 4096]); p.dma(tb1_sb[:, :], tb1_d.ap()[:, :])
    tb2_sb = p.sbuf([8, DM]); p.dma(tb2_sb[:, :], tb2_d.ap()[:, :])

    xt = [p.sbuf([128, TCH]) for _ in range(8)]
    xn = [p.sbuf([128, TCH]) for _ in range(8)]
    wsb = p.sbuf([128, 128])
    hacc = p.sbuf([128, 8])
    rcol = p.sbuf([128, 1])
    htile = p.sbuf([128, TCH])
    p.mset(hacc[:, :], 0.0)
    for c in range(NCH):
        c0 = c * TCH
        for f in range(8):
            p.dma(xt[f][:, :], x_d.ap()[f * 128:(f + 1) * 128, c0:c0 + TCH])
        emit_ln(p, [t[:, :] for t in xt], [t[:, :] for t in xn], ones_sb, TCH)
        for m in range(8):
            ps = p.psum()
            for k in range(8):
                p.dma(wsb[:, :], gw1_d.ap()[k * 128:(k + 1) * 128, m * 128:(m + 1) * 128])
                p.mm(ps[:, :TCH], wsb[:, :], xn[k][:, :], start=(k == 0), stop=(k == 7))
            p.act(htile[:, :], ps[:, :TCH], AF.Relu, bias=gb1_sb[:, m:m + 1])
            p.red(rcol[:, 0:1], htile[:, :], "add")
            p.tt(hacc[:, m:m + 1], hacc[:, m:m + 1], rcol[:, 0:1], OP.add)
    p.ts(hacc[:, :], hacc[:, :], 1.0 / L, OP.mult)
    g_ps = p.psum()
    for k in range(8):
        p.mm(g_ps[0:8, 0:1], gw2_sb[:, k * 8:(k + 1) * 8], hacc[:, k:k + 1],
             start=(k == 0), stop=(k == 7))
    g_sb = p.sbuf([8, 1])
    p.act(g_sb[:, :], g_ps[0:8, 0:1], AF.Identity, bias=gb2_sb[:, 0:1])
    tp = p.psum()
    p.trans(tp[0:1, 0:8], g_sb[:, :], eye_sb[0:8, 0:8])
    grow = p.sbuf([1, 8])
    p.copy(grow[:, :], tp[0:1, 0:8])
    gb_ps = p.psum()
    p.mm(gb_ps[:, 0:8], ones_sb[0:1, :], grow[:, :], start=True, stop=True)
    gb_sb = p.sbuf([128, 8])
    p.copy(gb_sb[:, :], gb_ps[:, 0:8])
    mb1_sb = p.sbuf([128, 32])
    for fb in range(32):
        ps = p.psum()
        p.mm(ps[0:128, 0:1], tb1_sb[:, fb * 128:(fb + 1) * 128], g_sb[:, :], start=True, stop=True)
        p.act(mb1_sb[:, fb:fb + 1], ps[0:128, 0:1], AF.Identity, bias=b1_sb[:, fb:fb + 1])
    mb2_sb = p.sbuf([128, 8])
    for fb in range(8):
        ps = p.psum()
        p.mm(ps[0:128, 0:1], tb2_sb[:, fb * 128:(fb + 1) * 128], g_sb[:, :], start=True, stop=True)
        p.act(mb2_sb[:, fb:fb + 1], ps[0:128, 0:1], AF.Identity, bias=b2_sb[:, fb:fb + 1])
    t0 = p.sbuf([128, 512]); t1 = p.sbuf([128, 512]); te = p.sbuf([128, 512])
    for (wsrc, twsrc, wdst, R) in [(w1_d, tw1_d, mw1_s, DM), (w2_d, tw2_d, mw2_s, 4096)]:
        C = 4096 if R == DM else DM
        for rt in range(R // 128):
            for fc in range(C // 512):
                p.dma(t0[:, :], wsrc.ap()[rt * 128:(rt + 1) * 128, fc * 512:(fc + 1) * 512])
                a, b = t0, t1
                for e in range(8):
                    p.dma(te[:, :], twsrc.ap()[e, rt * 128:(rt + 1) * 128, fc * 512:(fc + 1) * 512])
                    p.stt(b[:, :], te[:, :], gb_sb[:, e:e + 1], a[:, :], OP.mult, OP.add)
                    a, b = b, a
                p.dma(wdst.ap()[rt * 128:(rt + 1) * 128, fc * 512:(fc + 1) * 512], a[:, :])
    ut = [p.sbuf([128, TCH]) for _ in range(32)]
    tmp = p.sbuf([128, TCH])
    osb = p.sbuf([128, TCH])
    for c in range(NCH):
        c0 = c * TCH
        for f in range(8):
            p.dma(xt[f][:, :], x_d.ap()[f * 128:(f + 1) * 128, c0:c0 + TCH])
        emit_ln(p, [t[:, :] for t in xt], [t[:, :] for t in xn], ones_sb, TCH)
        def cb_u(m, ps):
            p.act(ut[m][:, :], ps, AF.Gelu, bias=mb1_sb[:, m:m + 1])
        emit_matmul_ff(p, mw1_s, [t[:, :] for t in xn], TCH, 8, 32, cb_u, wbuf=wsb)
        def cb_o(m, ps):
            p.act(osb[:, :], ps, AF.Identity, bias=mb2_sb[:, m:m + 1])
            p.dma(tmp[:, :], x_d.ap()[m * 128:(m + 1) * 128, c0:c0 + TCH])
            p.tt(osb[:, :], osb[:, :], tmp[:, :], OP.add)
            p.dma(xo_d.ap()[m * 128:(m + 1) * 128, c0:c0 + TCH], osb[:, :])
        emit_matmul_ff(p, mw2_s, [t[:, :] for t in ut], TCH, 32, 8, cb_o, wbuf=wsb)
    p.finalize()
    return p


_progs = {}


def _get(name):
    if name not in _progs:
        _progs[name] = globals()["build_" + name]()
    return _progs[name]


def _run(p, in_maps):
    return run_bass_kernel_spmd(p.nc, in_maps, core_ids=list(range(NCORES))).results


def _consts_np():
    f32 = np.float32
    ones = np.ones((128, 128), f32)
    eye = np.eye(128, dtype=f32)
    maskUT = np.where(np.arange(512)[None, :] >= np.arange(128)[:, None] + 128,
                      0.0, -1e30).astype(f32)
    maskBig = np.where(np.arange(896)[None, :] >= np.arange(128)[:, None] + 384,
                       0.0, -1e30).astype(f32)
    sel32 = np.repeat(np.eye(32, dtype=f32), 128, axis=1)
    iotarow = np.arange(L, dtype=f32).reshape(1, L)
    iotacol = np.arange(128, dtype=f32).reshape(128, 1)
    return ones, eye, maskUT, maskBig, sel32, iotarow, iotacol


def kernel(**inp):
    f32 = np.float32
    g = {k: np.asarray(v, dtype=f32) for k, v in inp.items()}
    ones, eye, maskUT, maskBig, sel32, iotarow, iotacol = _consts_np()
    A = np.ascontiguousarray

    xT = [A(g["L0_attn_u"][i].T) for i in range(NCORES)]

    W = g["L0_attn_in_proj_w"]
    wip = A(np.concatenate([W.T, W[::-1].T], axis=1))
    r = _run(_get("p1"), [dict(x=xT[i], wip=wip, ones=ones) for i in range(NCORES)])
    zx = [r[i]["zx"] for i in range(NCORES)]

    opwT = A(g["L0_attn_out_proj_w"].T)
    normw = g["L0_attn_norm_w"].reshape(16, 128)
    common = dict(normw=normw, opwT=opwT, ones=ones, eye=eye, maskUT=maskUT, sel32=sel32)
    fw = dict(convw=g["L0_attn_conv1_w"], convb=g["L0_attn_conv1_b"].reshape(18, 128),
              dtb=g["L0_attn_dt1_bias"].reshape(32, 1), alog=g["L0_attn_A_log1"].reshape(32, 1),
              drep=np.repeat(g["L0_attn_D1"], 64).reshape(16, 128), **common)
    bw = dict(convw=g["L0_attn_conv2_w"], convb=g["L0_attn_conv2_b"].reshape(18, 128),
              dtb=g["L0_attn_dt2_bias"].reshape(32, 1), alog=g["L0_attn_A_log2"].reshape(32, 1),
              drep=np.repeat(g["L0_attn_D2"], 64).reshape(16, 128), **common)
    r = _run(_get("p2"), [dict(zxb=A(zx[i][:4384]), **fw) for i in range(NCORES)])
    of = [r[i]["ob"] for i in range(NCORES)]
    r = _run(_get("p2"), [dict(zxb=A(zx[i][4384:]), **bw) for i in range(NCORES)])
    ob = [r[i]["ob"] for i in range(NCORES)]

    endwT = A(g["L0_attn_end_w"].T)
    endb = g["L0_attn_end_b"].reshape(8, 128)
    w1c = np.full((128, 1), float(g["L0_attn_w1"]), f32)
    w2c = np.full((128, 1), float(g["L0_attn_w2"]), f32)
    r = _run(_get("p4"), [dict(x=xT[i], of=of[i], obk=ob[i], endwT=endwT, endb=endb,
                               w1col=w1c, w2col=w2c) for i in range(NCORES)])
    x1 = [r[i]["xo"] for i in range(NCORES)]

    r = _run(_get("p5"), [dict(x=x1[i], wrT=A(g["L0_ffn_W_r"].T), wkT=A(g["L0_ffn_W_k"].T),
                               wvT=A(g["L0_ffn_W_v"].T), mur=g["L0_ffn_mu_r"].reshape(8, 128),
                               muk=g["L0_ffn_mu_k"].reshape(8, 128), ones=ones)
                          for i in range(NCORES)])
    x2 = [r[i]["xo"] for i in range(NCORES)]

    bkqv = g["L1_attn_b_kqv"]
    r = _run(_get("p6"), [dict(x=x2[i], wkqv=g["L1_attn_W_kqv"], bkq=bkqv[:2048].reshape(16, 128),
                               bvrow=bkqv[2048:].reshape(1, 1024), wproj=g["L1_attn_W_proj"],
                               bproj=g["L1_attn_b_proj"].reshape(8, 128),
                               slopes=g["L1_attn_slopes"].reshape(1, 16),
                               iotarow=iotarow, iotacol=iotacol, maskBig=maskBig, ones=ones)
                          for i in range(NCORES)])
    x3 = [r[i]["xo"] for i in range(NCORES)]

    r = _run(_get("p7"), [dict(x=x3[i], gw1T=A(g["L1_ffn_gate_w1"].T),
                               gb1=g["L1_ffn_gate_b1"].reshape(8, 128),
                               gw2T=A(g["L1_ffn_gate_w2"].T), gb2=g["L1_ffn_gate_b2"].reshape(8, 1),
                               w1T=A(g["L1_ffn_W1"].T), tw1T=A(g["L1_ffn_TW1"].transpose(0, 2, 1)),
                               w2T=A(g["L1_ffn_W2"].T), tw2T=A(g["L1_ffn_TW2"].transpose(0, 2, 1)),
                               tb1=g["L1_ffn_Tb1"], b1c=g["L1_ffn_b1"].reshape(32, 128),
                               tb2=g["L1_ffn_Tb2"], b2c=g["L1_ffn_b2"].reshape(8, 128),
                               ones=ones, eye=eye) for i in range(NCORES)])
    out = np.stack([r[i]["xo"].T for i in range(NCORES)], axis=0).astype(f32)
    return out
